# version 04
# baseline (speedup 1.0000x reference)
import numpy as np
from contextlib import ExitStack

import concourse.bass as bass
import concourse.mybir as mybir
from concourse.bass_utils import run_bass_kernel_spmd

F32 = mybir.dt.float32
BF16 = mybir.dt.bfloat16
I32 = mybir.dt.int32
AF = mybir.ActivationFunctionType
ALU = mybir.AluOpType

S = 8192
D = 1024
DC = 8
NH = 16
DH = 64
M = 11
CC = 128
NCH = S // CC

T1 = 256; NT1 = S // T1
T2 = 512; NT2 = S // T2
TF = 512; NTF = S // TF
T3A = 256; NT3A = S // T3A
T3B = 512; NT3B = S // T3B

BUILD_UPTO = "out"
N_CORES = 8

ENGS = ("t", "v", "s", "g", "y")

LN_EPS_LO = float(np.log(np.float32(1e-8)))
LN_EPS_HI = float(np.log1p(np.float64(-1e-8)))


class Chan:

    def __init__(self, bld, name, eng, depth=2, serial=False):
        self.eng = eng
        self.depth = 1 if serial else depth
        self.serial = serial
        self.sems = [bld.stack.enter_context(bld.nc.semaphore(f"ch_{name}{i}"))
                     for i in range(self.depth)]
        self.n = 0
        self.idx = len(bld.chans)
        bld.chans.append(self)


class Bld:

    def __init__(self, nc, stack):
        self.nc = nc
        self.stack = stack
        self.eng = {"t": nc.tensor, "v": nc.vector, "s": nc.scalar,
                    "g": nc.gpsimd, "y": nc.sync}
        self.NPROG = 4
        self.prog_sem = {e: [stack.enter_context(nc.semaphore(f"prog_{e}{k}"))
                             for k in range(self.NPROG)] for e in ENGS}
        self.chans = []
        self.ops = []
        self.last_w = {}
        self.readers = {}
        self.nbar = 0
        self.cur_bar = {e: None for e in ENGS}
        self.grp_inst = {}
        self.psum_rd = {}
        self.serial_on = False
        self.serial_prev = None

    def _dep_collect(self, r, w, eng=None):
        deps = set()
        for reg in r:
            t = self.last_w.get(reg)
            if t is not None:
                deps.add(t)
            if reg.startswith("p_"):
                for e2, g2 in self.psum_rd.get(reg, {}).items():
                    if e2 != eng:
                        deps.add(g2)
        for reg in w:
            t = self.last_w.get(reg)
            if t is not None:
                deps.add(t)
            for t2 in self.readers.get(reg, ()):
                deps.add(t2)
        return deps

    def _register(self, gid, r, w):
        for reg in w:
            self.last_w[reg] = gid
            self.readers[reg] = []
            if reg.startswith("p_"):
                self.psum_rd[reg] = {}
        for reg in r:
            self.readers.setdefault(reg, []).append(gid)

    def _add(self, rec, r, w, extra_deps=()):
        gid = len(self.ops)
        deps = self._dep_collect(r, w, eng=rec["e"])
        deps |= set(extra_deps)
        if self.cur_bar[rec["e"]] is not None:
            deps.add(self.cur_bar[rec["e"]])
        if self.serial_on and self.serial_prev is not None:
            deps.add(self.serial_prev)
        deps.discard(gid)
        rec["gid"] = gid
        rec["deps"] = deps
        rec["rr"] = tuple(r)
        rec["ww"] = tuple(w)
        for reg in r:
            if reg.startswith("p_"):
                self.psum_rd.setdefault(reg, {})[rec["e"]] = gid
        self.ops.append(rec)
        self._register(gid, r, w)
        if self.serial_on:
            self.serial_prev = gid
        return gid

    def op(self, eng, fn, r=(), w=(), tag="", c=600, grp=None, gstart=False):
        rec = dict(kind="op", e=eng, fn=fn, tag=tag, cost=c)
        if grp is not None:
            if gstart:
                self.grp_inst[grp] = self.grp_inst.get(grp, 0) + 1
            rec["grp"] = (grp, self.grp_inst.get(grp, 0))
        return self._add(rec, r, w)

    def dma(self, ch, out, in_, r=(), w=(), tag=""):
        nbytes = 4096
        try:
            nbytes = out.tensor.shape[0] * 1
        except Exception:
            pass
        return self._add(dict(kind="dma", e=ch.eng, ch=ch, out=out, in_=in_,
                              tag=tag, cost=3000), r, w)

    def barrier(self):
        toks = set()
        for rec in self.ops[::-1]:
            pass
        for t in self.last_w.values():
            toks.add(t)
        for rr in self.readers.values():
            toks.update(rr)
        self.nbar += 1
        hub = self._add(dict(kind="op", e="y", fn=lambda h: h.nop(),
                             tag=f"barhub{self.nbar}", cost=50), (), (),
                        extra_deps=toks)
        newbar = {}
        for e in ENGS:
            if e == "y":
                newbar[e] = hub
                continue
            newbar[e] = self._add(dict(kind="op", e=e, fn=lambda h: h.nop(),
                                       tag=f"bar{self.nbar}", cost=50), (), (),
                                  extra_deps={hub})
        self.cur_bar = newbar
        self.last_w = {}
        self.readers = {}
        self.psum_rd = {}

    SCHEDULE = True
    SCHED_ENGS = set(ENGS)
    WINDOW = None

    def finalize(self):
        import heapq
        n = len(self.ops)
        if not Bld.SCHEDULE:
            order = {e: [g for g in range(n) if self.ops[g]["e"] == e] for e in ENGS}
            finish = [0.0] * n
            self.sched_span = 0.0
            return self._materialize(order, finish)
        users = [[] for _ in range(n)]
        npred = [0] * n
        for rec in self.ops:
            g = rec["gid"]
            for d in rec["deps"]:
                users[d].append(g)
                npred[g] += 1
        finish = [0.0] * n
        ready = [0.0] * n
        eng_avail = {e: 0.0 for e in ENGS}
        heaps = {e: [] for e in ENGS}
        def hkey(g, r):
            if self.ops[g]["e"] in Bld.SCHED_ENGS:
                return (r, g)
            return (float(g), g)
        for rec in self.ops:
            if npred[rec["gid"]] == 0:
                heapq.heappush(heaps[rec["e"]], hkey(rec["gid"], 0.0))
        grp_size = {}
        for rec in self.ops:
            if "grp" in rec:
                grp_size[rec["grp"]] = grp_size.get(rec["grp"], 0) + 1
        active_grp = [None, 0]
        minu_p = [0]
        sched_flag = [False] * n
        order = {e: [] for e in ENGS}
        scheduled = 0
        stash = []
        while scheduled < n:
            best = None
            for e in ENGS:
                h = heaps[e]
                if not h:
                    continue
                if e == "t" and active_grp[0] is not None:
                    found = None
                    for item in h:
                        if self.ops[item[1]].get("grp") == active_grp[0]:
                            found = item
                            break
                    if found is None:
                        continue
                    r, g = found
                else:
                    r, g = h[0]
                rr = ready[g]
                st = max(eng_avail[e], rr)
                if best is None or (st, g) < best[0:2]:
                    best = (st, g, e)
            if Bld.WINDOW is not None and best is not None:
                minu = min(minu_p[0], n - 1)
                while sched_flag[minu_p[0]] and minu_p[0] < n - 1:
                    minu_p[0] += 1
                lim = minu_p[0] + Bld.WINDOW
                if best[1] > lim:
                    best2 = None
                    for e2 in ENGS:
                        for item in heaps[e2]:
                            g2 = item[1]
                            if g2 <= lim:
                                st2 = max(eng_avail[e2], ready[g2])
                                if best2 is None or (st2, g2) < best2[0:2]:
                                    best2 = (st2, g2, e2)
                    if best2 is not None:
                        best = best2
            assert best is not None, "scheduler stuck"
            st, g, e = best
            heaps[e].remove(heaps[e][0] if heaps[e][0][1] == g else
                            next(item for item in heaps[e] if item[1] == g))
            heapq.heapify(heaps[e])
            rec = self.ops[g]
            sched_flag[g] = True
            if e == "t" and "grp" in rec:
                k = rec["grp"]
                if active_grp[0] is None:
                    active_grp = [k, grp_size[k]]
                active_grp[1] -= 1
                if active_grp[1] <= 0:
                    active_grp = [None, 0]
            order[e].append(g)
            if rec["kind"] == "dma":
                eng_avail[e] = st + 150.0
                finish[g] = st + rec["cost"]
            else:
                eng_avail[e] = st + rec["cost"]
                finish[g] = eng_avail[e]
            scheduled += 1
            for u in users[g]:
                ready[u] = max(ready[u], finish[g])
                npred[u] -= 1
                if npred[u] == 0:
                    heapq.heappush(heaps[self.ops[u]["e"]], hkey(u, ready[u]))
        self.sched_span = max(finish) if n else 0.0
        return self._materialize(order, finish)

    def _materialize(self, order, finish):
        n = len(self.ops)
        pos_in_eng = {}
        for e in ENGS:
            for i, g in enumerate(order[e]):
                pos_in_eng[g] = i
        slot_of = {}
        slot_count = {}
        dma_wait_val = {}
        for e in ENGS:
            for g in order[e]:
                rec = self.ops[g]
                if rec["kind"] == "dma":
                    ch = rec["ch"]
                    k = slot_count.get((ch.idx,), 0)
                    slot_count[(ch.idx,)] = k + 1
                    sl = k % ch.depth
                    slot_of[g] = sl
                    dma_wait_val[g] = k // ch.depth + 1
                    rec["slot"] = sl
                    rec["seqv"] = k // ch.depth + 1
        inc_needed = [False] * n
        for rec in self.ops:
            for d in rec["deps"]:
                if self.ops[d]["kind"] == "op":
                    inc_needed[d] = True
        incval = {}
        incsem = {}
        for e in ENGS:
            cnt = 0
            for g in order[e]:
                if inc_needed[g]:
                    incval[g] = cnt // self.NPROG + 1
                    incsem[g] = cnt % self.NPROG
                    cnt += 1
        streams = {e: [] for e in ENGS}
        know = {e: {} for e in ENGS}
        nwaits = 0
        for e in ENGS:
            for g in order[e]:
                rec = self.ops[g]
                waits = {}
                for d in rec["deps"]:
                    drec = self.ops[d]
                    if drec["kind"] == "op":
                        if drec["e"] == "t" and e == "t":
                            continue
                        src = ("E", drec["e"], incsem[d])
                        v = incval[d]
                    else:
                        src = ("C", drec["ch"].idx, slot_of[d])
                        v = dma_wait_val[d]
                    if know[e].get(src, 0) < v:
                        waits[src] = max(waits.get(src, 0), v)
                for srck, v in sorted(waits.items(), key=lambda kv: str(kv[0])):
                    if srck[0] == "E":
                        streams[e].append(("w", self.prog_sem[srck[1]][srck[2]], v))
                    else:
                        streams[e].append(
                            ("w", self.chans[srck[1]].sems[srck[2]], 16 * v))
                    know[e][srck] = v
                    nwaits += 1
                inc = (self.prog_sem[e][incsem[g]]
                       if (rec["kind"] == "op" and inc_needed[g]) else None)
                streams[e].append(("x", rec, inc))
        self.streams = streams
        self.n_waits = nwaits

    def run_block(self):
        bld = self

        def runner(e):
            def go(_h):
                h = bld.eng[e]
                for item in bld.streams[e]:
                    if item[0] == "w":
                        h.wait_ge(item[1], item[2])
                    else:
                        rec, psem = item[1], item[2]
                        if rec["kind"] == "dma":
                            inst = h.dma_start(out=rec["out"], in_=rec["in_"])
                            inst.then_inc(rec["ch"].sems[rec["slot"]], 16)
                            if rec["ch"].serial:
                                h.wait_ge(rec["ch"].sems[0], 16 * rec["seqv"])
                        else:
                            inst = rec["fn"](h)
                            if psem is not None:
                                inst.then_inc(psem, 1)
            return go

        with bld.nc.Block() as block:
            block.sync(runner("y"))
            block.tensor(runner("t"))
            block.vector(runner("v"))
            block.scalar(runner("s"))
            block.gpsimd(runner("g"))


class Scope:
    def __init__(self, nc):
        self.nc = nc
        self.guards = []

    def sb(self, name, shape, dt):
        g = self.nc.sbuf_tensor(name, shape, dt)
        t = g.__enter__()
        self.guards.append(g)
        import os
        if os.environ.get("KDBG"):
            print("SB", name, self.nc.sbuf_bytes_remaining)
        return t

    def ps(self, name, shape, dt=F32):
        g = self.nc.psum_tensor(name, shape, dt)
        t = g.__enter__()
        self.guards.append(g)
        return t

    def close(self):
        for g in reversed(self.guards):
            g.__exit__(None, None, None)
        self.guards = []


def RG(t):
    return t.name


def build_program():
    nc = bass.Bass("TRN2", target_bir_lowering=False)
    stack = ExitStack()
    B = Bld(nc, stack)
    sb = lambda name, shape, dt: stack.enter_context(nc.sbuf_tensor(name, shape, dt))
    ps = lambda name, shape, dt=F32: stack.enter_context(nc.psum_tensor(name, shape, dt))

    xin_h = nc.dram_tensor("x0T", [D, S], F32, kind="ExternalInput").ap()
    w_inT = nc.dram_tensor("w_inT", [D, 3 * D], F32, kind="ExternalInput").ap()
    w_outT = nc.dram_tensor("w_outT", [D, D], F32, kind="ExternalInput").ap()
    betaP_h = nc.dram_tensor("betaP", [128, 88], F32, kind="ExternalInput").ap()
    qwT = nc.dram_tensor("qwT", [6, D, D], I32, kind="ExternalInput").ap()
    srT = nc.dram_tensor("srT", [6, D, D], F32, kind="ExternalInput").ap()
    la_h = nc.dram_tensor("la", [6, 32, D], F32, kind="ExternalInput").ap()
    lbT_h = nc.dram_tensor("lbT", [6, 32, D], F32, kind="ExternalInput").ap()
    biasF_h = nc.dram_tensor("biasF", [6, 128, 8], F32, kind="ExternalInput").ap()
    gamF_h = nc.dram_tensor("gamF", [3, 128, 8], F32, kind="ExternalInput").ap()
    betF_h = nc.dram_tensor("betF", [3, 128, 8], F32, kind="ExternalInput").ap()
    bwinT_h = nc.dram_tensor("bwinT", [D, 2 * D], F32, kind="ExternalInput").ap()
    bwoutT_h = nc.dram_tensor("bwoutT", [D, D], F32, kind="ExternalInput").ap()
    wbT_h = nc.dram_tensor("wbT", [D, 16], F32, kind="ExternalInput").ap()
    wcT_h = nc.dram_tensor("wcT", [16, D], F32, kind="ExternalInput").ap()
    binT_h = nc.dram_tensor("binT", [128, 16], F32, kind="ExternalInput").ap()
    boutT_h = nc.dram_tensor("boutT", [128, 8], F32, kind="ExternalInput").ap()
    lngT_h = nc.dram_tensor("lngT", [128, 8], F32, kind="ExternalInput").ap()
    lnbT_h = nc.dram_tensor("lnbT", [128, 8], F32, kind="ExternalInput").ap()
    dskT_h = nc.dram_tensor("dskT", [128, 8], F32, kind="ExternalInput").ap()
    ldtT_h = nc.dram_tensor("ldtT", [128, 8], F32, kind="ExternalInput").ap()
    alogT_h = nc.dram_tensor("alogT", [128, 128], F32, kind="ExternalInput").ap()
    identF_h = nc.dram_tensor("identF", [128, 128], F32, kind="ExternalInput").ap()
    triuF_h = nc.dram_tensor("triuF", [128, 128], F32, kind="ExternalInput").ap()
    onesk_h = nc.dram_tensor("onesk", [128, 128], F32, kind="ExternalInput").ap()
    onesr_h = nc.dram_tensor("onesr", [1, 128], F32, kind="ExternalInput").ap()
    iotaB_h = nc.dram_tensor("iotaB", [128, 132], F32, kind="ExternalInput").ap()
    iotaP_h = nc.dram_tensor("iotaP", [128, 2], F32, kind="ExternalInput").ap()
    out_h = nc.dram_tensor("out", [D, S], F32, kind="ExternalOutput").ap()

    qspill = (out_h if BUILD_UPTO == "q" else
              nc.dram_tensor("qspill", [D, S], F32, kind="Internal").ap())
    vspill = nc.dram_tensor("vspill", [S, D], BF16, kind="Internal").ap()
    x1s = nc.dram_tensor("x1s", [D, S], F32, kind="Internal").ap()
    x2s = nc.dram_tensor("x2s", [D, S], F32, kind="Internal").ap()
    x3s = nc.dram_tensor("x3s", [D, S], F32, kind="Internal").ap()
    xbs = nc.dram_tensor("xbs", [D, S], F32, kind="Internal").ap()
    zsp = nc.dram_tensor("zsp", [D, S], BF16, kind="Internal").ap()
    xpsp = nc.dram_tensor("xpsp", [D, S], BF16, kind="Internal").ap()
    wfb = nc.dram_tensor("wfb", [6, D, D], BF16, kind="Internal").ap()
    lrsp = nc.dram_tensor("lrsp", [128, 128], F32, kind="Internal").ap()

    chL = Chan(B, "L", "y", depth=2)
    chL2 = Chan(B, "L2", "y", depth=2)
    chL3 = Chan(B, "L3", "s", depth=2)
    chL4 = Chan(B, "L4", "g", depth=2)
    chS = Chan(B, "S", "s", depth=2)
    chS2 = Chan(B, "S2", "s", depth=2)
    chW = Chan(B, "W", "y", depth=2)
    chP = Chan(B, "P", "y", serial=True)
    chP2 = Chan(B, "P2", "s", serial=True)
    chG = Chan(B, "G", "g", serial=True)

    cons = {}
    for nm, src, shp in [("identF", identF_h, [128, 128]),
                         ("triuF", triuF_h, [128, 128]),
                         ("iotaB", iotaB_h, [128, 132]),
                         ("iotaP", iotaP_h, [128, 2]),
                         ("onesr", onesr_h, [1, 128]),
                         ("betaP", betaP_h, [128, 88]),
                         ("binT", binT_h, [128, 16]),
                         ("boutT", boutT_h, [128, 8]),
                         ("lngT", lngT_h, [128, 8]),
                         ("lnbT", lnbT_h, [128, 8]),
                         ("dskT", dskT_h, [128, 8]),
                         ("ldtT", ldtT_h, [128, 8]),
                         ("alogT", alogT_h, [128, 128])]:
        t = sb("c_" + nm, shp, F32)
        cons[nm] = t
        B.dma(chP, t[:], src[:], w=[RG(t)], tag="ld_" + nm)
    identF = cons["identF"]
    biasF = sb("biasF_t", [128, 6 * 8], F32)
    B.dma(chP, biasF[:].rearrange("p (i k) -> p i k", i=6),
          biasF_h.rearrange("i p k -> p i k"), w=[RG(biasF)])
    gamF = sb("gamF_t", [128, 3 * 8], F32)
    B.dma(chP, gamF[:].rearrange("p (i k) -> p i k", i=3),
          gamF_h.rearrange("i p k -> p i k"), w=[RG(gamF)])
    betF = sb("betF_t", [128, 3 * 8], F32)
    B.dma(chP, betF[:].rearrange("p (i k) -> p i k", i=3),
          betF_h.rearrange("i p k -> p i k"), w=[RG(betF)])
    onesK = sb("onesK", [128, 128], BF16)
    B.dma(chG, onesK[:], onesk_h[:], w=[RG(onesK)])
    epsT = sb("epsT", [128, 1], F32)
    B.op("v", lambda h: h.memset(epsT[:], 1e-5), w=[RG(epsT)])
    k1024 = sb("k1024", [128, 1], F32)
    B.op("v", lambda h: h.memset(k1024[:], 1.0 / 1024.0), w=[RG(k1024)])

    wrkS = [[sb(f"wrk{k}_{j}", [128, 512], F32) for j in range(5)] for k in range(2)]
    wrk = wrkS[0]
    WRKR = [RG(w) for w in wrk]

    def ln_fm(xf, T, pmu, pmur, pm2, pm2r, vtmp, vtr, c16, s16,
              out_bf=None, sub_dst=None, gam=None, bet=None, gbr="",
              f32_to_bf=None, tag="ln"):
        csc = (224 + DC * T) / 1.2
        cvv = (58 + T) / 0.96
        cmm = T / 2.4 + 120
        """Feature-major LayerNorm of xf [128, DC*T] fp32.
        If sub_dst given, (x-mu)*rstd is written there (xf preserved),
        else in-place in xf. If gam given, affine applied in-place after.
        If out_bf given, final result copied/written to it in bf16."""
        xfr = RG(xf)
        c16r, s16r = RG(c16), RG(s16)
        B.op("s", lambda h: h.copy(c16[:, 0:DC * T], xf[:, 0:DC * T]),
             r=[xfr], w=[c16r], tag=tag + "c", c=csc)
        B.op("s", lambda h: h.activation(s16[:, 0:DC * T], c16[:, 0:DC * T], AF.Square),
             r=[c16r], w=[s16r], tag=tag + "q", c=csc)
        for dc in range(DC):
            B.op("t", lambda h, dc=dc: h.matmul(
                pmu, onesK[:], c16[:, dc * T:(dc + 1) * T],
                start=(dc == 0), stop=(dc == DC - 1), skip_group_check=True),
                r=[RG(onesK), c16r], w=[pmur], tag=tag + "su", c=cmm,
                grp=pmur, gstart=(dc == 0))
        if sub_dst is None:
            sub_dst = xf
        sdr = RG(sub_dst)
        for dc in range(DC):
            B.op("v", lambda h, dc=dc: h.tensor_sub(
                sub_dst[:, dc * T:(dc + 1) * T], xf[:, dc * T:(dc + 1) * T], pmu),
                r=[xfr, pmur], w=[sdr], tag=tag + "sb", c=cvv)
        for dc in range(DC):
            B.op("t", lambda h, dc=dc: h.matmul(
                pm2, onesK[:], s16[:, dc * T:(dc + 1) * T],
                start=(dc == 0), stop=(dc == DC - 1), skip_group_check=True),
                r=[RG(onesK), s16r], w=[pm2r], tag=tag + "s2", c=cmm,
                grp=pm2r, gstart=(dc == 0))
        B.op("s", lambda h: h.activation(vtmp, pmu, AF.Square), r=[pmur], w=[vtr], tag=tag + "m2")
        B.op("v", lambda h: h.tensor_sub(pm2, pm2, vtmp), r=[vtr], w=[pm2r], tag=tag + "vr")
        B.op("s", lambda h: h.activation(pm2, pm2, AF.Ln, bias=epsT[:, 0:1]),
             r=[RG(epsT)], w=[pm2r], tag=tag + "ln", c=cvv)
        B.op("s", lambda h: h.activation(pm2, pm2, AF.Exp, scale=-0.5),
             w=[pm2r], tag=tag + "ex", c=cvv)
        if gam is None and out_bf is not None and f32_to_bf is None:
            for dc in range(DC):
                B.op("v", lambda h, dc=dc: h.tensor_mul(
                    out_bf[:, dc * T:(dc + 1) * T], sub_dst[:, dc * T:(dc + 1) * T], pm2),
                    r=[sdr, pm2r], w=[RG(out_bf)], tag=tag + "ml")
            return
        for dc in range(DC):
            B.op("v", lambda h, dc=dc: h.tensor_mul(
                sub_dst[:, dc * T:(dc + 1) * T], sub_dst[:, dc * T:(dc + 1) * T], pm2),
                r=[pm2r], w=[sdr], tag=tag + "ml")
        if gam is not None:
            for dc in range(DC):
                B.op("v", lambda h, dc=dc: h.tensor_scalar(
                    sub_dst[:, dc * T:(dc + 1) * T], sub_dst[:, dc * T:(dc + 1) * T],
                    gam[:, dc:dc + 1], bet[:, dc:dc + 1], ALU.mult, ALU.add),
                    r=[gbr], w=[sdr], tag=tag + "af", c=cvv * 0.7)
        if out_bf is not None:
            B.op("s", lambda h: h.copy(out_bf[:, 0:DC * T], sub_dst[:, 0:DC * T]),
                 r=[sdr], w=[RG(out_bf)], tag=tag + "cb", c=csc)

    def cheb_features(src, src_reg, coloff, width, slots, slots_reg,
                      slot_stride=None, slot_off=0, ws=0, tag="chf"):
        if slot_stride is None:
            slot_stride = width
        wrk = wrkS[ws]
        WRKR = [RG(w) for w in wrk]
        cv = (58 + width) / 0.96
        cs_ = (224 + width) / 1.2
        w2, w3, w4, w5, u = [w[:, 0:width] for w in wrk]
        ksl = src[:, coloff:coloff + width]
        sl = lambda m: slots[:, m * slot_stride + slot_off:
                             m * slot_stride + slot_off + width]
        B.op("s", lambda h: h.copy(sl(1), ksl), r=[src_reg], w=[slots_reg], tag=tag + "1", c=cs_)
        B.op("s", lambda h: h.activation(u, ksl, AF.Square), r=[src_reg], w=[WRKR[4]], tag=tag + "2s", c=cs_)
        B.op("v", lambda h: h.tensor_scalar(w2, u, 2.0, 1.0, ALU.mult, ALU.subtract),
             r=[WRKR[4]], w=[WRKR[0]], tag=tag + "2", c=cv)
        B.op("s", lambda h: h.copy(sl(2), w2), r=[WRKR[0]], w=[slots_reg], tag=tag + "2c", c=cs_)
        B.op("v", lambda h: h.tensor_mul(u, ksl, w2), r=[src_reg, WRKR[0]], w=[WRKR[4]], tag=tag + "3m", c=cv)
        B.op("v", lambda h: h.scalar_tensor_tensor(w3, u, 2.0, ksl, ALU.mult, ALU.subtract),
             r=[WRKR[4], src_reg], w=[WRKR[1]], tag=tag + "3", c=cv)
        B.op("s", lambda h: h.copy(sl(3), w3), r=[WRKR[1]], w=[slots_reg], tag=tag + "3c", c=cs_)
        B.op("s", lambda h: h.activation(u, w2, AF.Square), r=[WRKR[0]], w=[WRKR[4]], tag=tag + "4s", c=cs_)
        B.op("v", lambda h: h.tensor_scalar(w4, u, 2.0, 1.0, ALU.mult, ALU.subtract),
             r=[WRKR[4]], w=[WRKR[2]], tag=tag + "4", c=cv)
        B.op("g", lambda h: h.tensor_copy(sl(4), w4), r=[WRKR[2]], w=[slots_reg], tag=tag + "4c", c=cv * 2.2)
        B.op("v", lambda h: h.tensor_mul(u, w2, w3), r=[WRKR[0], WRKR[1]], w=[WRKR[4]], tag=tag + "5m", c=cv)
        B.op("v", lambda h: h.scalar_tensor_tensor(w5, u, 2.0, ksl, ALU.mult, ALU.subtract),
             r=[WRKR[4], src_reg], w=[WRKR[3]], tag=tag + "5", c=cv)
        B.op("g", lambda h: h.tensor_copy(sl(5), w5), r=[WRKR[3]], w=[slots_reg], tag=tag + "5c", c=cv * 2.2)
        B.op("s", lambda h: h.activation(u, w3, AF.Square), r=[WRKR[1]], w=[WRKR[4]], tag=tag + "6s", c=cs_)
        B.op("v", lambda h: h.tensor_scalar(sl(6), u, 2.0, 1.0, ALU.mult, ALU.subtract),
             r=[WRKR[4]], w=[slots_reg], tag=tag + "6", c=cv)
        B.op("v", lambda h: h.tensor_mul(u, w3, w4), r=[WRKR[1], WRKR[2]], w=[WRKR[4]], tag=tag + "7m", c=cv)
        B.op("v", lambda h: h.scalar_tensor_tensor(sl(7), u, 2.0, ksl, ALU.mult, ALU.subtract),
             r=[WRKR[4], src_reg], w=[slots_reg], tag=tag + "7", c=cv)
        B.op("s", lambda h: h.activation(u, w4, AF.Square), r=[WRKR[2]], w=[WRKR[4]], tag=tag + "8s", c=cs_)
        B.op("v", lambda h: h.tensor_scalar(sl(8), u, 2.0, 1.0, ALU.mult, ALU.subtract),
             r=[WRKR[4]], w=[slots_reg], tag=tag + "8", c=cv)
        B.op("v", lambda h: h.tensor_mul(u, w4, w5), r=[WRKR[2], WRKR[3]], w=[WRKR[4]], tag=tag + "9m", c=cv)
        B.op("v", lambda h: h.scalar_tensor_tensor(sl(9), u, 2.0, ksl, ALU.mult, ALU.subtract),
             r=[WRKR[4], src_reg], w=[slots_reg], tag=tag + "9", c=cv)
        B.op("s", lambda h: h.activation(u, w5, AF.Square), r=[WRKR[3]], w=[WRKR[4]], tag=tag + "10s", c=cs_)
        B.op("v", lambda h: h.tensor_scalar(sl(10), u, 2.0, 1.0, ALU.mult, ALU.subtract),
             r=[WRKR[4]], w=[slots_reg], tag=tag + "10", c=cv)

    x0T_v = xin_h.rearrange("(c p) t -> p c t", c=DC)
    qsp_v = qspill.rearrange("(c p) t -> p c t", c=DC)
    vsp_v = vspill.rearrange("(i q p) d -> i p q d", q=2, p=128)

    def fm_view(ap):
        return ap.rearrange("(c p) t -> p c t", c=DC)

    sc_kv = Scope(nc)
    kvsb = [sc_kv.sb(f"kvsb{p}", [128, M * DH], F32) for p in range(8)]
    sc_wi = Scope(nc)
    wi = sc_wi.sb("wi", [128, DC * 3 * D], BF16)
    sc0 = Scope(nc)
    wtmp = [sc0.sb(f"wtmp{j}", [128, 3 * D], F32) for j in range(2)]
    for dc in range(DC):
        t = wtmp[dc % 2]
        B.dma(chP, t[:], w_inT[dc * 128:(dc + 1) * 128, :], w=[RG(t)], tag="wild")
        B.op("v", lambda h, t=t, dc=dc: h.tensor_scalar(
            wi[:, dc * 3 * D: dc * 3 * D + 2 * D], t[:, 0:2 * D], 0.125, None, ALU.mult),
            r=[RG(t)], w=[RG(wi)], tag="wiqk")
        B.op("v", lambda h, t=t, dc=dc: h.tensor_copy(
            wi[:, dc * 3 * D + 2 * D: (dc + 1) * 3 * D], t[:, 2 * D:3 * D]),
            r=[RG(t)], w=[RG(wi)], tag="wiv")
    B.barrier()
    sc0.close()

    sc1 = Scope(nc)
    xin = [sc1.sb(f"xinA{j}", [128, DC * T1], F32) for j in range(2)]
    xc16 = sc1.sb("xc16A", [128, DC * T1], BF16)
    sq16 = sc1.sb("sq16A", [128, DC * T1], BF16)
    xab = [sc1.sb(f"xabA{j}", [128, DC * T1], BF16) for j in range(2)]
    qsp = sc1.sb("qspA", [128, DC * T1], F32)
    ktok = sc1.sb("ktokA", [128, 2 * D], F32)
    vtok = [sc1.sb("vtokA0", [128, 2 * D], BF16)] * 2
    tkF = [sc1.sb(f"tkFA{j}", [128, M * D], BF16) for j in range(2)]
    vvar = sc1.sb("vvarA", [128, 512], F32)
    p_st = sc1.ps("p_stA", [128, T1])
    p_st2 = sc1.ps("p_st2A", [128, T1])
    p_mm = [sc1.ps(f"p_mmA{j}", [128, 512]) for j in range(2)]
    p_kv = [(sc1.ps(f"p_kvA{j}", [128, 6 * DH]), sc1.ps(f"p_kvB{j}", [128, 5 * DH]))
            for j in range(2)]

    for j in range(2):
        B.op("v", lambda h, j=j: h.memset(tkF[j][:, 0:D], 1.0), w=[RG(tkF[j])], tag="tkones")

    for i in range(NT1):
        b = i % 2
        xb, xa, vt = xin[b], xab[b], vtok[b]
        pmu = p_st[:, 0:T1]
        pm2 = p_st2[:, 0:T1]
        vtm = vvar[:, b * T1:(b + 1) * T1]
        B.dma(chL, xb[:].rearrange("p (c n) -> p c n", c=DC),
              x0T_v[:, :, i * T1:(i + 1) * T1], w=[RG(xb)], tag=f"x0ld{i}")
        ln_fm(xb, T1, pmu, RG(p_st), pm2, RG(p_st2), vtm, f"vvar:{b}",
              xc16, sq16, out_bf=xa, tag=f"l0_{i}")
        for oc in range(DC):
            pq = p_mm[oc % 2]
            for dc in range(DC):
                B.op("t", lambda h, oc=oc, dc=dc, pq=pq, xa=xa: h.matmul(
                    pq[:, 0:T1],
                    wi[:, dc * 3 * D + oc * 128: dc * 3 * D + (oc + 1) * 128],
                    xa[:, dc * T1:(dc + 1) * T1],
                    start=(dc == 0), stop=(dc == DC - 1)),
                    r=[RG(wi), RG(xa)], w=[RG(pq)], tag=f"qmm{i}", c=210,
                    grp=RG(pq), gstart=(dc == 0))
            B.op("v", lambda h, oc=oc, pq=pq: h.tensor_scalar(
                qsp[:, oc * T1:(oc + 1) * T1], pq[:, 0:T1], -1.0, 1.0, ALU.max, ALU.min),
                r=[RG(pq)], w=[RG(qsp)], tag=f"qev{i}", c=330)
        B.dma(chS, qsp_v[:, :, i * T1:(i + 1) * T1],
              qsp[:].rearrange("p (c n) -> p c n", c=DC), r=[RG(qsp)], tag=f"qsp{i}")
        for ts in range(2):
            for jj in range(4):
                pk = p_mm[jj % 2]
                for dc in range(DC):
                    B.op("t", lambda h, ts=ts, jj=jj, dc=dc, pk=pk, xa=xa: h.matmul(
                        pk[:, 0:512],
                        xa[:, dc * T1 + ts * 128: dc * T1 + (ts + 1) * 128],
                        wi[:, dc * 3 * D + D + jj * 512: dc * 3 * D + D + (jj + 1) * 512],
                        start=(dc == 0), stop=(dc == DC - 1)),
                        r=[RG(wi), RG(xa)], w=[RG(pk)], tag=f"kvmm{i}", c=330,
                        grp=RG(pk), gstart=(dc == 0))
                if jj < 2:
                    B.op("v", lambda h, ts=ts, jj=jj, pk=pk: h.tensor_scalar(
                        ktok[:, ts * D + jj * 512: ts * D + (jj + 1) * 512],
                        pk[:, 0:512], -1.0, 1.0, ALU.max, ALU.min),
                        r=[RG(pk)], w=[RG(ktok)], tag=f"kev{i}", c=600)
                else:
                    B.op("s", lambda h, ts=ts, jj=jj, pk=pk, vt=vt: h.copy(
                        vt[:, ts * D + (jj - 2) * 512: ts * D + (jj - 1) * 512], pk[:, 0:512]),
                        r=[RG(pk)], w=[RG(vt)], tag=f"vev{i}", c=600)
        B.dma(chS2, vsp_v[i], vt[:].rearrange("p (q d) -> p q d", q=2),
              r=[RG(vt)], tag=f"vsp{i}")
        for ts in range(2):
            for half in range(2):
                cheb_features(ktok, RG(ktok), ts * D + half * 512, 512,
                              tkF[ts], RG(tkF[ts]), slot_stride=D,
                              slot_off=half * 512, ws=(2 * i + ts) % 2,
                              tag=f"tk{i}_{ts}_{half}")
        for pr in range(8):
            pga, pgb = p_kv[pr % 2]
            for (m0, nm, pg) in [(0, 6, pga), (6, 5, pgb)]:
                for hh in range(2):
                    for ts in range(2):
                        tkv = tkF[ts][:].rearrange("p (m d) -> p m d", m=M)
                        B.op("t", lambda h, pr=pr, ts=ts, m0=m0, nm=nm, pg=pg, hh=hh, tkv=tkv, vt=vt: h.matmul(
                            pg[hh * 64:(hh + 1) * 64, 0:nm * DH],
                            vt[:, ts * D + pr * 128 + hh * 64: ts * D + pr * 128 + (hh + 1) * 64],
                            tkv[:, m0:m0 + nm, pr * 128 + hh * 64: pr * 128 + (hh + 1) * 64],
                            start=(ts == 0), stop=(ts == 1),
                            tile_position=(0, hh * 64), skip_group_check=True),
                            r=[RG(tkF[ts]), RG(vt)], w=[RG(pg)], tag=f"kve{i}_{pr}", c=260,
                            grp=RG(pg), gstart=(hh == 0 and ts == 0))
            eng = "v"
            if i == 0:
                B.op(eng, lambda h, pr=pr, pga=pga: h.tensor_copy(
                    kvsb[pr][:, 0:6 * DH], pga[:, 0:6 * DH]),
                    r=[RG(pga)], w=[RG(kvsb[pr])], tag=f"kva{i}_{pr}", c=560)
                B.op(eng, lambda h, pr=pr, pgb=pgb: h.tensor_copy(
                    kvsb[pr][:, 6 * DH:M * DH], pgb[:, 0:5 * DH]),
                    r=[RG(pgb)], w=[RG(kvsb[pr])], tag=f"kvb{i}_{pr}", c=500)
            else:
                B.op(eng, lambda h, pr=pr, pga=pga: h.tensor_add(
                    kvsb[pr][:, 0:6 * DH], kvsb[pr][:, 0:6 * DH], pga[:, 0:6 * DH]),
                    r=[RG(pga)], w=[RG(kvsb[pr])], tag=f"kva{i}_{pr}", c=560)
                B.op(eng, lambda h, pr=pr, pgb=pgb: h.tensor_add(
                    kvsb[pr][:, 6 * DH:M * DH], kvsb[pr][:, 6 * DH:M * DH], pgb[:, 0:5 * DH]),
                    r=[RG(pgb)], w=[RG(kvsb[pr])], tag=f"kvb{i}_{pr}", c=500)
    B.barrier()
    sc1.close()
    sc_wi.close()
    if BUILD_UPTO == "q":
        sc_kv.close()
        B.finalize()
        B.run_block()
        stack.close()
        return nc
    if BUILD_UPTO == "kv":
        for pr in range(8):
            B.dma(chP2, out_h[0:128, pr * 704:(pr + 1) * 704], kvsb[pr][:],
                  r=[RG(kvsb[pr])], tag=f"kvdump{pr}")
        B.barrier()
        sc_kv.close()
        B.finalize()
        B.run_block()
        stack.close()
        return nc

    sc2 = Scope(nc)
    betaP = cons["betaP"]
    kvT = [sc2.sb(f"kvT{p}", [128, M * DH], BF16) for p in range(8)]
    tmpB = sc2.sb("tmpB", [64, 2 * 64], BF16)
    p_tp = [sc2.ps(f"p_tpF{j}", [128, 64]) for j in range(2)]
    p_po = [sc2.ps(f"p_poF{j}", [128, T2]) for j in range(2)]
    p_mm = [sc2.ps(f"p_mmF{j}", [128, 512]) for j in range(2)]
    for pr in range(8):
        for j in range(M):
            pt = p_tp[j % 2]
            B.op("t", lambda h, pr=pr, j=j, pt=pt: h.transpose(
                pt[0:64, :], kvsb[pr][0:64, j * 64:(j + 1) * 64], identF[0:64, 0:64]),
                r=[RG(kvsb[pr]), RG(identF)], w=[RG(pt)], tag=f"kvt{pr}_{j}a")
            B.op("v", lambda h, pr=pr, j=j, pt=pt: h.tensor_scalar(
                kvT[pr][0:64, j * 64:(j + 1) * 64], pt[0:64, :],
                betaP[0:64, pr * 11 + j: pr * 11 + j + 1], None, ALU.mult),
                r=[RG(pt), RG(betaP)], w=[RG(kvT[pr])], tag=f"kvs{pr}_{j}a")
            B.op("t", lambda h, pr=pr, j=j, pt=pt: h.transpose(
                pt[0:64, :], kvsb[pr][64:128, j * 64:(j + 1) * 64],
                identF[64:128, 64:128], tile_position=(64, 0)),
                r=[RG(kvsb[pr]), RG(identF)], w=[RG(pt)], tag=f"kvt{pr}_{j}b")
            B.op("v", lambda h, pr=pr, j=j, pt=pt: h.tensor_scalar(
                tmpB[:, (j % 2) * 64:(j % 2) * 64 + 64], pt[0:64, :],
                betaP[64:128, pr * 11 + j: pr * 11 + j + 1], None, ALU.mult),
                r=[RG(pt), RG(betaP)], w=[RG(tmpB) + str(j % 2)], tag=f"kvs{pr}_{j}b")
            B.dma(chG, kvT[pr][64:128, j * 64:(j + 1) * 64],
                  tmpB[:, (j % 2) * 64:(j % 2) * 64 + 64],
                  r=[RG(tmpB) + str(j % 2)], w=[RG(kvT[pr])], tag=f"kvd{pr}_{j}")

    wo = sc2.sb("wo", [128, DC * D], BF16)
    wotmp = [sc2.sb(f"wotmp{j}", [128, D], F32) for j in range(2)]
    for dc in range(DC):
        t = wotmp[dc % 2]
        B.dma(chP, t[:], w_outT[dc * 128:(dc + 1) * 128, :], w=[RG(t)], tag="wold")
        B.op("v", lambda h, t=t, dc=dc: h.tensor_copy(wo[:, dc * D:(dc + 1) * D], t[:]),
             r=[RG(t)], w=[RG(wo)], tag="woc")

    qin = [sc2.sb(f"qinF{j}", [128, DC * T2], F32) for j in range(2)]
    x0f = [sc2.sb(f"x0fF{j}", [128, DC * T2], F32) for j in range(2)]
    tqp = [sc2.sb(f"tqpF{j}", [128, M * T2], BF16) for j in range(2)]
    aob = [sc2.sb(f"aobF{j}", [128, DC * T2], BF16) for j in range(2)]
    for j in range(2):
        B.op("v", lambda h, j=j: h.memset(tqp[j][:, 0:T2], 1.0), w=[RG(tqp[j])], tag="tqones")

    x1_tgt = out_h if BUILD_UPTO == "x1" else x1s
    x1_v = fm_view(x1_tgt)

    for i in range(NT2):
        b = i % 2
        qf, xf, ao = qin[b], x0f[b], aob[b]
        B.dma(chL, qf[:].rearrange("p (c n) -> p c n", c=DC),
              qsp_v[:, :, i * T2:(i + 1) * T2], w=[RG(qf)], tag=f"qld{i}")
        B.dma(chL2, xf[:].rearrange("p (c n) -> p c n", c=DC),
              x0T_v[:, :, i * T2:(i + 1) * T2], w=[RG(xf)], tag=f"x0ld2_{i}")
        for pr in range(8):
            tq = tqp[pr % 2]
            cheb_features(qf, RG(qf), pr * T2, T2, tq, RG(tq), ws=pr % 2,
                          tag=f"tq{i}_{pr}")
            po = p_po[pr % 2]
            tqv = tq[:].rearrange("p (m n) -> p m n", m=M)
            for j in range(M):
                for hh in range(2):
                    B.op("t", lambda h, pr=pr, j=j, hh=hh, po=po, tqv=tqv: h.matmul(
                        po[hh * 64:(hh + 1) * 64, 0:T2],
                        kvT[pr][hh * 64:(hh + 1) * 64, j * 64:(j + 1) * 64],
                        tqv[hh * 64:(hh + 1) * 64, j, :],
                        start=(j == 0), stop=(j == M - 1),
                        tile_position=(hh * 64, hh * 64), skip_group_check=True),
                        r=[RG(kvT[pr]), RG(tq)], w=[RG(po)], tag=f"oe{i}_{pr}", c=280,
                        grp=RG(po), gstart=(j == 0 and hh == 0))
            B.op("s", lambda h, pr=pr, po=po, ao=ao: h.copy(
                ao[:, pr * T2:(pr + 1) * T2], po[:, 0:T2]),
                r=[RG(po)], w=[RG(ao)], tag=f"aoev{i}_{pr}", c=640)
        for oc in range(DC):
            pw = p_mm[oc % 2]
            for fc in range(DC):
                B.op("t", lambda h, oc=oc, fc=fc, pw=pw, ao=ao: h.matmul(
                    pw[:, 0:T2],
                    wo[:, fc * D + oc * 128: fc * D + (oc + 1) * 128],
                    ao[:, fc * T2:(fc + 1) * T2],
                    start=(fc == 0), stop=(fc == DC - 1)),
                    r=[RG(wo), RG(ao)], w=[RG(pw)], tag=f"womm{i}", c=330,
                    grp=RG(pw), gstart=(fc == 0))
            B.op("v", lambda h, oc=oc, pw=pw, xf=xf: h.tensor_add(
                xf[:, oc * T2:(oc + 1) * T2], xf[:, oc * T2:(oc + 1) * T2], pw[:, 0:T2]),
                r=[RG(pw)], w=[RG(xf)], tag=f"x1r{i}", c=690)
        B.dma(chS, x1_v[:, :, i * T2:(i + 1) * T2],
              xf[:].rearrange("p (c n) -> p c n", c=DC), r=[RG(xf)], tag=f"x1st{i}")
    B.barrier()
    sc2.close()
    sc_kv.close()

    if BUILD_UPTO == "x1":
        B.finalize()
        B.run_block()
        stack.close()
        return nc

    sc3 = Scope(nc)
    qtmp = [sc3.sb(f"qtmp{j}", [128, D], I32) for j in range(2)]
    stmp = [sc3.sb(f"stmp{j}", [128, D], F32) for j in range(2)]
    qf32 = [sc3.sb(f"qf32{j}", [128, D], F32) for j in range(2)]
    wchk = [sc3.sb(f"wchk{j}", [128, D], BF16) for j in range(2)]
    la16 = sc3.sb("la16", [32, D], BF16)
    lb16 = sc3.sb("lb16", [32, D], BF16)
    p_lo = [sc3.ps(f"p_loB{j}", [128, 1024]) for j in range(2)]
    wfb_v = wfb.rearrange("l (c p) o -> l c p o", p=128)
    for li in range(6):
        B.dma(chG, la16[:], la_h[li], w=[RG(la16)], tag=f"lald{li}")
        B.dma(chG, lb16[:], lbT_h[li], w=[RG(lb16)], tag=f"lbld{li}")
        for dc in range(DC):
            p = dc % 2
            qt, st_, qf, wc = qtmp[p], stmp[p], qf32[p], wchk[p]
            B.dma(chL, qt[:], qwT[li, dc * 128:(dc + 1) * 128, :], w=[RG(qt)], tag=f"qwld{li}")
            B.dma(chL2, st_[:], srT[li, dc * 128:(dc + 1) * 128, :], w=[RG(st_)], tag=f"srld{li}")
            B.op("v", lambda h, qt=qt, qf=qf: h.tensor_copy(qf[:], qt[:]),
                 r=[RG(qt)], w=[RG(qf)], tag=f"qcv{li}")
            B.op("v", lambda h, qf=qf: h.tensor_scalar(qf[:], qf[:], 1.0 / 7.5, 1.0,
                                                       ALU.mult, ALU.subtract),
                 w=[RG(qf)], tag=f"qsc{li}")
            B.op("v", lambda h, st_=st_, qf=qf: h.tensor_mul(qf[:], qf[:], st_[:]),
                 r=[RG(st_)], w=[RG(qf)], tag=f"qmu{li}")
            for half in range(2):
                B.op("t", lambda h, dc=dc, half=half, pl=p_lo[p]: h.matmul(
                    pl[:, half * 512:(half + 1) * 512],
                    la16[:, dc * 128:(dc + 1) * 128],
                    lb16[:, half * 512:(half + 1) * 512],
                    start=True, stop=True, skip_group_check=True),
                    r=[RG(la16), RG(lb16)], w=[RG(p_lo[p]) + str(half)], tag=f"lomm{li}")
            for half in range(2):
                B.op("v", lambda h, qf=qf, wc=wc, half=half, pl=p_lo[p]: h.tensor_add(
                    wc[:, half * 512:(half + 1) * 512],
                    qf[:, half * 512:(half + 1) * 512],
                    pl[:, half * 512:(half + 1) * 512]),
                    r=[RG(qf), RG(p_lo[p]) + str(half)], w=[RG(wc)], tag=f"wef{li}")
            B.dma(chS, wfb_v[li, dc], wc[:], r=[RG(wc)], tag=f"wfbst{li}")
    B.barrier()
    sc3.close()

    def ffn_phase(src_v, dst_v, scf, tagp):
        wbuf = [scf.sb(f"wb{tagp}{j}", [128, DC * D], BF16) for j in range(2)]
        p_stf = [scf.ps(f"p_st{tagp}{j}", [128, TF]) for j in range(2)]
        p_st2f = [scf.ps(f"p_st2{tagp}{j}", [128, TF]) for j in range(2)]
        p_mm = [scf.ps(f"p_mm{tagp}{j}", [128, 512]) for j in range(2)]
        xt = [scf.sb(f"x{tagp}{j}", [128, DC * TF], F32) for j in range(2)]
        hF = [scf.sb(f"hF{tagp}{j}", [128, DC * TF], F32) for j in range(2)]
        h16 = [scf.sb(f"h16{tagp}{j}", [128, DC * TF], BF16) for j in range(2)]
        yF = [scf.sb(f"yF{tagp}{j}", [128, DC * TF], F32) for j in range(2)]
        c16 = [scf.sb(f"c16{tagp}{j}", [128, DC * TF], BF16) for j in range(2)]
        s16 = [scf.sb(f"s16{tagp}{j}", [128, DC * TF], BF16) for j in range(2)]
        vv = scf.sb(f"vv{tagp}", [128, 1024], F32)

        nwl = [0]

        def wload():
            k = nwl[0]
            li = k % 6
            B.dma(chW, wbuf[k % 2][:].rearrange("p (c o) -> p c o", c=DC),
                  wfb.rearrange("l (c p) o -> l p c o", p=128)[li],
                  w=[RG(wbuf[k % 2])], tag=f"{tagp}wl{k}")
            nwl[0] += 1

        wload()
        wload()
        for i in range(NTF):
            b = i % 2
            xf, hf, hb, yf = xt[b], hF[b], h16[b], yF[b]
            pmu = p_stf[b][:, 0:TF]
            pmur = f"p_st{tagp}{b}"
            pm2 = p_st2f[b][:, 0:TF]
            pm2r = f"p_st2{tagp}{b}"
            vtm = vv[:, b * TF:(b + 1) * TF]
            B.dma(chL, xf[:].rearrange("p (c n) -> p c n", c=DC),
                  src_v[:, :, i * TF:(i + 1) * TF], w=[RG(xf)], tag=f"{tagp}xld{i}")
            ln_fm(xf, TF, pmu, pmur, pm2, pm2r, vtm, f"vv{tagp}{b}", c16[b], s16[b],
                  out_bf=hb, sub_dst=hf, f32_to_bf=True, tag=f"{tagp}l{i}")
            for blk in range(3):
                li0, li1 = 2 * blk, 2 * blk + 1
                ku = 6 * i + 2 * blk
                w0, w1 = wbuf[ku % 2], wbuf[(ku + 1) % 2]
                for oc in range(DC):
                    pf = p_mm[oc % 2]
                    for dc in range(DC):
                        B.op("t", lambda h, oc=oc, dc=dc, pf=pf, w0=w0, hb=hb: h.matmul(
                            pf[:, 0:TF],
                            w0[:, dc * D + oc * 128: dc * D + (oc + 1) * 128],
                            hb[:, dc * TF:(dc + 1) * TF],
                            start=(dc == 0), stop=(dc == DC - 1)),
                            r=[RG(w0), RG(hb)], w=[RG(pf)], tag=f"{tagp}l1mm{i}", c=330,
                            grp=RG(pf), gstart=(dc == 0))
                    B.op("v", lambda h, oc=oc, pf=pf, li0=li0, yf=yf: h.tensor_scalar(
                        yf[:, oc * TF:(oc + 1) * TF], pf[:, 0:TF],
                        biasF[:, li0 * 8 + oc: li0 * 8 + oc + 1], None, ALU.add),
                        r=[RG(pf), RG(biasF)], w=[RG(yf)], tag=f"{tagp}l1ev{i}", c=400)
                wload()
                ln_fm(yf, TF, pmu, pmur, pm2, pm2r, vtm, f"vv{tagp}{b}", c16[b], s16[b],
                      gam=gamF[:, blk * 8:(blk + 1) * 8], bet=betF[:, blk * 8:(blk + 1) * 8],
                      gbr=RG(gamF), tag=f"{tagp}li{i}_{blk}")
                B.op("s", lambda h, yf=yf, hb=hb: h.activation(
                    hb[:, 0:DC * TF], yf[:, 0:DC * TF], AF.Gelu),
                    r=[RG(yf)], w=[RG(hb)], tag=f"{tagp}gel{i}_{blk}", c=3600)
                for oc in range(DC):
                    pf = p_mm[oc % 2]
                    for dc in range(DC):
                        B.op("t", lambda h, oc=oc, dc=dc, pf=pf, w1=w1, hb=hb: h.matmul(
                            pf[:, 0:TF],
                            w1[:, dc * D + oc * 128: dc * D + (oc + 1) * 128],
                            hb[:, dc * TF:(dc + 1) * TF],
                            start=(dc == 0), stop=(dc == DC - 1)),
                            r=[RG(w1), RG(hb)], w=[RG(pf)], tag=f"{tagp}l2mm{i}", c=330,
                            grp=RG(pf), gstart=(dc == 0))
                    B.op("v", lambda h, oc=oc, pf=pf, li1=li1, hf=hf: h.scalar_tensor_tensor(
                        hf[:, oc * TF:(oc + 1) * TF], pf[:, 0:TF],
                        biasF[:, li1 * 8 + oc: li1 * 8 + oc + 1],
                        hf[:, oc * TF:(oc + 1) * TF], ALU.add, ALU.add),
                        r=[RG(pf), RG(biasF)], w=[RG(hf)], tag=f"{tagp}l2ev{i}", c=690)
                wload()
                if blk < 2:
                    B.op("s", lambda h, hf=hf, hb=hb: h.copy(hb[:, 0:DC * TF], hf[:, 0:DC * TF]),
                         r=[RG(hf)], w=[RG(hb)], tag=f"{tagp}hc{i}_{blk}", c=3600)
            for dc in range(DC):
                B.op("v", lambda h, dc=dc, xf=xf, hf=hf: h.tensor_add(
                    xf[:, dc * TF:(dc + 1) * TF], xf[:, dc * TF:(dc + 1) * TF],
                    hf[:, dc * TF:(dc + 1) * TF]),
                    r=[RG(hf)], w=[RG(xf)], tag=f"{tagp}res{i}", c=690)
            B.dma(chS, dst_v[:, :, i * TF:(i + 1) * TF],
                  xf[:].rearrange("p (c n) -> p c n", c=DC), r=[RG(xf)], tag=f"{tagp}xst{i}")

    x2_tgt = out_h if BUILD_UPTO == "x2" else x2s
    scf1 = Scope(nc)
    ffn_phase(fm_view(x1s), fm_view(x2_tgt), scf1, "f1")
    B.barrier()
    scf1.close()
    if BUILD_UPTO == "x2":
        B.finalize()
        B.run_block()
        stack.close()
        return nc

    B.serial_on = True
    sc4p = Scope(nc)
    wbo = sc4p.sb("wbo", [128, DC * D], BF16)
    wbB = sc4p.sb("wbB", [128, DC * 16], BF16)
    wcb = sc4p.sb("wcb", [16, D], BF16)
    bti = sc4p.sb("bti", [128, 16], F32)
    xmT = sc4p.sb("xmT", [16, S], BF16)
    sc4q = Scope(nc)
    wbi = sc4q.sb("wbi", [128, DC * 2 * D], BF16)
    PWTb = sc4q.sb("PWTb", [128, 128 * 132], BF16)
    Tsb = sc4q.sb("Tsb", [128, 16 * 128], BF16)
    acol = sc4q.sb("acol", [128, 128], F32)
    lrt = sc4q.sb("lrt", [128, 128], F32)
    nlr = sc4q.sb("nlr", [128, 128], F32)
    dtt = sc4q.sb("dtt", [128, 8], F32)
    dtmB = sc4q.sb("dtmB", [128, 1], F32)
    sc4 = Scope(nc)
    PWTnb = sc4.sb("PWTnb", [128, 128 * 128], BF16)
    btmp = [sc4.sb(f"btmpB{j}", [128, 2 * D], F32) for j in range(2)]
    wbt = sc4.sb("wbtB", [128, DC * 16], F32)
    wct32 = sc4.sb("wct32", [16, D], F32)
    mm1 = sc4.sb("mm1", [1, 1], F32)
    p_sm = sc4.ps("p_smC", [128, 16])
    ldtT = cons["ldtT"]; lnbT = cons["lnbT"]; lngT = cons["lngT"]
    alogT = cons["alogT"]; iotaB = cons["iotaB"]; iotaP = cons["iotaP"]
    onesr = cons["onesr"]; triuF = cons["triuF"]; binT = cons["binT"]

    B.op("s", lambda h: h.activation(dtt[:], ldtT[:], AF.Exp), r=[RG(ldtT)], w=[RG(dtt)], tag="dt")
    B.op("v", lambda h: h.tensor_scalar(dtt[:], dtt[:], 1e-4, 1.0, ALU.max, ALU.min),
         w=[RG(dtt)], tag="dtc")
    B.op("v", lambda h: h.tensor_reduce(wrk[0][:, 0:1],
                                        dtt[:], mybir.AxisListType.X, ALU.add),
         r=[RG(dtt)], w=[WRKR[0]], tag="dtr")
    B.op("t", lambda h: h.matmul(p_sm[0:1, 0:1], wrk[0][:, 0:1], k1024[:, 0:1],
                                 start=True, stop=True, skip_group_check=True),
         r=[WRKR[0], RG(k1024)], w=[RG(p_sm)], tag="dtm")
    B.op("v", lambda h: h.tensor_copy(mm1[:], p_sm[0:1, 0:1]), r=[RG(p_sm)], w=[RG(mm1)], tag="dtmc")
    B.op("t", lambda h: h.matmul(p_sm[:, 1:2], onesr[0:1, :], mm1[0:1, 0:1],
                                 start=True, stop=True, skip_group_check=True),
         r=[RG(mm1), RG(onesr)], w=[RG(p_sm)], tag="dtb")
    B.op("v", lambda h: h.tensor_copy(dtmB[:], p_sm[:, 1:2]), r=[RG(p_sm)], w=[RG(dtmB)], tag="dtbc")

    B.op("s", lambda h: h.activation(lrt[:], alogT[:], AF.Exp), r=[RG(alogT)], w=[RG(lrt)], tag="lre")
    for dc in range(DC):
        B.op("v", lambda h, dc=dc: h.tensor_scalar(
            lrt[:, dc * 16:(dc + 1) * 16], lrt[:, dc * 16:(dc + 1) * 16],
            dtt[:, dc:dc + 1], -1.0, ALU.mult, ALU.mult),
            r=[RG(dtt)], w=[RG(lrt)], tag="lrm")
    B.op("v", lambda h: h.tensor_scalar(lrt[:], lrt[:], LN_EPS_LO, LN_EPS_HI, ALU.max, ALU.min),
         w=[RG(lrt)], tag="lrc")
    B.op("v", lambda h: h.tensor_scalar(nlr[:], lrt[:], -1.0, None, ALU.mult),
         r=[RG(lrt)], w=[RG(nlr)], tag="nlr")

    for dc in range(DC):
        for s_ in range(16):
            cs = dc * 16 + s_
            B.op("v", lambda h, cs=cs: h.tensor_scalar(
                wrk[4][:, 0:132], iotaB[:], lrt[:, cs:cs + 1], None, ALU.mult),
                r=[RG(iotaB), RG(lrt)], w=[WRKR[4]], tag="pwu")
            B.op("s", lambda h, cs=cs: h.activation(
                PWTb[:, cs * 132:(cs + 1) * 132], wrk[4][:, 0:132], AF.Exp),
                r=[WRKR[4]], w=[RG(PWTb)], tag="pwe")
            B.op("s", lambda h, cs=cs: h.activation(
                acol[:, cs:cs + 1], wrk[4][:, 128:129], AF.Exp),
                r=[WRKR[4]], w=[RG(acol)], tag="pwa")
            B.op("v", lambda h, cs=cs: h.tensor_scalar(
                wrk[4][:, 0:128], iotaB[:, 0:128], nlr[:, cs:cs + 1], None, ALU.mult),
                r=[RG(iotaB), RG(nlr)], w=[WRKR[4]], tag="pwnu")
            B.op("s", lambda h, cs=cs: h.activation(
                PWTnb[:, cs * 128:(cs + 1) * 128], wrk[4][:, 0:128], AF.Exp),
                r=[WRKR[4]], w=[RG(PWTnb)], tag="pwne")
    p_T = [sc4.ps(f"p_TC{j}", [128, 128]) for j in range(2)]
    for s_ in range(16):
        pt = p_T[s_ % 2]
        for dc in range(DC):
            cs = dc * 16 + s_
            B.op("t", lambda h, cs=cs, pt=pt, dc=dc: h.matmul(
                pt[:], PWTnb[:, cs * 128:(cs + 1) * 128],
                PWTb[:, cs * 132: cs * 132 + 128],
                start=(dc == 0), stop=(dc == DC - 1)),
                r=[RG(PWTnb), RG(PWTb)], w=[RG(pt)], tag=f"tsm{s_}",
                grp=RG(pt), gstart=(dc == 0))
        B.op("v", lambda h, s_=s_, pt=pt: h.tensor_mul(
            Tsb[:, s_ * 128:(s_ + 1) * 128], pt[:], triuF[:]),
            r=[RG(pt), RG(triuF)], w=[RG(Tsb)], tag=f"tse{s_}")

    for dc in range(DC):
        t = btmp[dc % 2]
        B.dma(chL, t[:], bwinT_h[dc * 128:(dc + 1) * 128, :], w=[RG(t)], tag="bwild")
        for oc in range(16):
            B.op("t", lambda h, t=t, dc=dc, oc=oc: h.matmul(
                p_sm[:, oc:oc + 1], t[:, oc * 128:(oc + 1) * 128], lnbT[:, dc:dc + 1],
                start=(dc == 0), stop=(dc == DC - 1), skip_group_check=True),
                r=[RG(t), RG(lnbT)], w=[RG(p_sm)], tag="btimm",
                grp="btig", gstart=(dc == 0 and oc == 0))
        B.op("v", lambda h, t=t, dc=dc: h.tensor_scalar(
            wbi[:, dc * 2 * D:(dc + 1) * 2 * D], t[:], lngT[:, dc:dc + 1], None, ALU.mult),
            r=[RG(t), RG(lngT)], w=[RG(wbi)], tag="wbig")
    B.op("v", lambda h: h.tensor_add(bti[:], p_sm[:], binT[:]),
         r=[RG(p_sm), RG(binT)], w=[RG(bti)], tag="bti")
    B.dma(chG, wbo[:].rearrange("p (c o) -> p c o", c=DC),
          bwoutT_h.rearrange("(c p) o -> p c o", c=DC), w=[RG(wbo)], tag="wbold")
    B.dma(chP, wbt[:].rearrange("p (c k) -> p c k", c=DC),
          wbT_h.rearrange("(c p) k -> p c k", c=DC), w=[RG(wbt)], tag="wbtld")
    for dc in range(DC):
        B.op("v", lambda h, dc=dc: h.tensor_scalar(
            wbB[:, dc * 16:(dc + 1) * 16], wbt[:, dc * 16:(dc + 1) * 16],
            dtmB[:, 0:1], None, ALU.mult),
            r=[RG(wbt), RG(dtmB)], w=[RG(wbB)], tag="wbb")
    B.dma(chP, wct32[:], wcT_h[:], w=[RG(wct32)], tag="wcld")
    B.op("v", lambda h: h.tensor_scalar(wcb[:], wct32[:], 1.0 / 1024.0, None, ALU.mult),
         r=[RG(wct32)], w=[RG(wcb)], tag="wcb")
    B.barrier()
    sc4.close()

    sc5 = Scope(nc)
    u_all = sc5.sb("u_all", [128, 16 * NCH], BF16)
    h_all = sc5.sb("h_all", [128, 128 * (NCH + 1)], BF16)
    xmf = sc5.sb("xmf", [128, 16 * NCH], F32)
    sc5a = Scope(nc)
    xt3 = [sc5a.sb(f"xt3{j}", [128, DC * T3A], F32) for j in range(2)]
    xbF = sc5a.sb("xbF", [128, DC * T3A], F32)
    xnF = sc5a.sb("xnF", [128, DC * T3A], F32)
    xn16 = sc5a.sb("xn16", [128, DC * T3A], BF16)
    xp16 = sc5a.sb("xp16", [128, DC * T3A], BF16)
    z16 = sc5a.sb("z16", [128, DC * T3A], BF16)
    c16b = sc5a.sb("c16b", [128, DC * T3A], BF16)
    s16b = sc5a.sb("s16b", [128, DC * T3A], BF16)
    vv3 = sc5a.sb("vv3", [128, 512], F32)
    bp_s = sc5a.sb("bp_s", [16, T3A], F32)
    p_st3 = sc5a.ps("p_st3", [128, T3A])
    p_st23 = sc5a.ps("p_st23", [128, T3A])
    p_mm = [sc5a.ps(f"p_mm3{j}", [128, 512]) for j in range(2)]
    p_bp = sc5a.ps("p_bp3", [128, T3A])

    x2_v = fm_view(x2s)
    xbs_v = fm_view(xbs)
    zsp_v = fm_view(zsp)
    xpsp_v = fm_view(xpsp)

    for i in range(NT3A):
        b = i % 2
        xf = xt3[b]
        pmu = p_st3[:, 0:T3A]
        pmur = RG(p_st3)
        pm2 = p_st23[:, 0:T3A]
        pm2r = RG(p_st23)
        vtm = vv3[:, b * T3A:(b + 1) * T3A]
        B.dma(chL, xf[:].rearrange("p (c n) -> p c n", c=DC),
              x2_v[:, :, i * T3A:(i + 1) * T3A], w=[RG(xf)], tag=f"3xld{i}")
        ln_fm(xf, T3A, pmu, pmur, pm2, pm2r, vtm, f"vv3{b}", c16b, s16b,
              sub_dst=xbF, f32_to_bf=True, tag=f"3lo{i}")
        B.dma(chS, xbs_v[:, :, i * T3A:(i + 1) * T3A],
              xbF[:].rearrange("p (c n) -> p c n", c=DC), r=[RG(xbF)], tag=f"3xbst{i}")
        ln_fm(xbF, T3A, pmu, pmur, pm2, pm2r, vtm, f"vv3{b}", c16b, s16b,
              out_bf=xn16, sub_dst=xnF, tag=f"3li{i}")
        for oc in range(16):
            pf = p_mm[oc % 2]
            for dc in range(DC):
                B.op("t", lambda h, oc=oc, dc=dc, pf=pf: h.matmul(
                    pf[:, 0:T3A],
                    wbi[:, dc * 2 * D + oc * 128: dc * 2 * D + (oc + 1) * 128],
                    xn16[:, dc * T3A:(dc + 1) * T3A],
                    start=(dc == 0), stop=(dc == DC - 1)),
                    r=[RG(wbi), RG(xn16)], w=[RG(pf)], tag=f"3wimm{i}", c=210,
                    grp=RG(pf), gstart=(dc == 0))
            if oc < 8:
                B.op("v", lambda h, oc=oc, pf=pf: h.tensor_scalar(
                    xp16[:, oc * T3A:(oc + 1) * T3A], pf[:, 0:T3A],
                    bti[:, oc:oc + 1], None, ALU.add),
                    r=[RG(pf), RG(bti)], w=[RG(xp16)], tag=f"3xpev{i}", c=330)
            else:
                B.op("s", lambda h, oc=oc, pf=pf: h.activation(
                    z16[:, (oc - 8) * T3A:(oc - 7) * T3A], pf[:, 0:T3A], AF.Silu,
                    bias=bti[:, oc:oc + 1]),
                    r=[RG(pf), RG(bti)], w=[RG(z16)], tag=f"3zev{i}", c=450)
        B.dma(chS2, xpsp_v[:, :, i * T3A:(i + 1) * T3A],
              xp16[:].rearrange("p (c n) -> p c n", c=DC), r=[RG(xp16)], tag=f"3xpst{i}")
        B.dma(chL4, zsp_v[:, :, i * T3A:(i + 1) * T3A],
              z16[:].rearrange("p (c n) -> p c n", c=DC), r=[RG(z16)], tag=f"3zst{i}")
        for dc in range(DC):
            B.op("t", lambda h, dc=dc: h.matmul(
                p_bp[0:16, 0:T3A], wbB[:, dc * 16:(dc + 1) * 16],
                xp16[:, dc * T3A:(dc + 1) * T3A],
                start=(dc == 0), stop=(dc == DC - 1)),
                r=[RG(wbB), RG(xp16)], w=[RG(p_bp)], tag=f"3bpmm{i}", c=210,
                grp=RG(p_bp), gstart=(dc == 0))
        B.op("v", lambda h: h.tensor_copy(bp_s[:], p_bp[0:16, 0:T3A]),
             r=[RG(p_bp)], w=[RG(bp_s)], tag=f"3bpev{i}")
        for csub in range(T3A // 128):
            c = i * (T3A // 128) + csub
            po = p_bp[:, T3A + csub * 16: T3A + (csub + 1) * 16] if False else None
            B.op("t", lambda h, csub=csub: h.transpose(
                p_mm[csub % 2][:, 0:16], bp_s[:, csub * 128:(csub + 1) * 128],
                identF[0:16, 0:16]),
                r=[RG(bp_s), RG(identF)], w=[f"p_mm3{csub % 2}"], tag=f"3bpt{i}_{csub}")
            B.op("s", lambda h, c=c, csub=csub: h.copy(
                u_all[:].rearrange("p (k n) -> p k n", k=16)[:, :, c],
                p_mm[csub % 2][:, 0:16]),
                r=[f"p_mm3{csub % 2}"], w=[RG(u_all)], tag=f"3uev{i}_{csub}")
    B.barrier()
    sc5a.close()

    sc6 = Scope(nc)
    p_inj = sc6.ps("p_injM", [128, 512])
    p_xm = [sc6.ps(f"p_xmM{j}", [128, 512]) for j in range(2)]
    p_bc = sc6.ps("p_bcM", [128, 132])
    p_TM = [sc6.ps(f"p_TM{j}", [128, 128]) for j in range(2)]
    PWrev = sc6.sb("PWrev", [128, 128 * 128], BF16)
    lrow_c = sc6.sb("lrow_c", [1, 2048], F32)
    B.op("t", lambda h: h.transpose(p_TM[0][:], lrt[:], identF[:]),
         r=[RG(lrt), RG(identF)], w=[RG(p_TM[0])], tag="lrtT")
    B.op("v", lambda h: h.tensor_copy(wrk[0][:, 0:128], p_TM[0][:]),
         r=[RG(p_TM[0])], w=[WRKR[0]], tag="lrTev")
    lrT2 = wrk[0]
    B.dma(chP2, lrsp, lrT2[:, 0:128], r=[WRKR[0]], tag="lrst")
    for cs in range(128):
        if cs % 16 == 0:
            B.dma(chP, lrow_c[:],
                  lrsp.rearrange("a b -> (a b)")[None, cs * 128:(cs + 16) * 128],
                  r=[], w=[RG(lrow_c)], tag=f"lrld{cs}")
        B.op("t", lambda h, cs=cs: h.matmul(
            p_bc[:, 0:128], onesr[0:1, :], lrow_c[0:1, (cs % 16) * 128:(cs % 16 + 1) * 128],
            start=True, stop=True, skip_group_check=True),
            r=[RG(lrow_c), RG(onesr)], w=[RG(p_bc)], tag=f"lrb{cs}")
        B.op("v", lambda h, cs=cs: h.tensor_scalar(
            wrk[4][:, 0:128], p_bc[:, 0:128], iotaP[:, 1:2], None, ALU.mult),
            r=[RG(p_bc), RG(iotaP)], w=[WRKR[4]], tag=f"pwru{cs}")
        B.op("s", lambda h, cs=cs: h.activation(
            PWrev[:, cs * 128:(cs + 1) * 128], wrk[4][:, 0:128], AF.Exp),
            r=[WRKR[4]], w=[RG(PWrev)], tag=f"pwre{cs}")
    hv = h_all[:].rearrange("p (k n) -> p k n", k=128)
    B.op("v", lambda h: h.memset(h_all[:], 0.0), w=[RG(h_all)], tag="hz")
    for s_ in range(16):
        for dc in range(DC):
            cs = dc * 16 + s_
            B.op("t", lambda h, s_=s_, dc=dc, cs=cs: h.matmul(
                p_inj[:, dc * 64:(dc + 1) * 64],
                PWrev[:, cs * 128:(cs + 1) * 128],
                u_all[:].rearrange("p (k n) -> p k n", k=16)[:, s_, :],
                start=True, stop=True, skip_group_check=True),
                r=[RG(PWrev), RG(u_all)], w=[RG(p_inj)], tag=f"inj{s_}_{dc}")
        for dc in range(DC):
            cs = dc * 16 + s_
            B.op("v", lambda h, s_=s_, dc=dc, cs=cs: h.tensor_tensor_scan(
                hv[:, cs, 1:NCH + 1],
                acol[:, cs:cs + 1].broadcast_to((128, NCH)),
                p_inj[:, dc * 64:(dc + 1) * 64], 0.0, ALU.mult, ALU.add),
                r=[RG(acol), RG(p_inj)], w=[RG(h_all)], tag=f"scan{s_}_{dc}")
        pxm = p_xm[s_ % 2]
        for dc in range(DC):
            cs = dc * 16 + s_
            B.op("t", lambda h, s_=s_, dc=dc, cs=cs: h.matmul(
                pxm[:, 0:NCH],
                PWTb[:, cs * 132 + 1: cs * 132 + 129],
                hv[:, cs, 0:NCH],
                start=(dc == 0), stop=False, skip_group_check=True),
                r=[RG(PWTb), RG(h_all)], w=[RG(pxm)], tag=f"xmi{s_}_{dc}",
                grp=RG(pxm), gstart=(dc == 0))
        B.op("t", lambda h, s_=s_, pxm=pxm: h.matmul(
            pxm[:, 0:NCH], Tsb[:, s_ * 128:(s_ + 1) * 128],
            u_all[:].rearrange("p (k n) -> p k n", k=16)[:, s_, :],
            start=False, stop=True, skip_group_check=True),
            r=[RG(Tsb), RG(u_all)], w=[RG(pxm)], tag=f"xma{s_}",
            grp=RG(pxm))
        B.op("v", lambda h, s_=s_, pxm=pxm: h.tensor_copy(
            xmf[:].rearrange("p (k n) -> p k n", k=16)[:, s_, :], pxm[:, 0:NCH]),
            r=[RG(pxm)], w=[RG(xmf)], tag=f"xme{s_}")
    for c in range(NCH):
        pt = p_tp[c % 2]
        B.op("t", lambda h, c=c: h.transpose(
            p_TM[c % 2][0:16, 0:128],
            xmf[:].rearrange("p (k n) -> p k n", k=16)[:, :, c], identF[:]),
            r=[RG(xmf), RG(identF)], w=[RG(p_TM[c % 2])], tag=f"xmt{c}")
        B.op("s", lambda h, c=c: h.copy(
            xmT[0:16, c * 128:(c + 1) * 128], p_TM[c % 2][0:16, 0:128]),
            r=[RG(p_TM[c % 2])], w=[RG(xmT)], tag=f"xmtc{c}")
    if BUILD_UPTO == "ssm":
        B.dma(chP2, out_h[0:128, 0:128], lrt[:], r=[RG(lrt)], tag="d1")
        B.dma(chP2, out_h[0:128, 128:256], acol[:], r=[RG(acol)], tag="d2")
        B.dma(chG, out_h[0:128, 256:2304], Tsb[:], r=[RG(Tsb)], tag="d3")
        B.dma(chG, out_h[0:128, 2304:3328], u_all[:], r=[RG(u_all)], tag="d4")
        B.dma(chP2, out_h[0:128, 3328:4352], xmf[:], r=[RG(xmf)], tag="d5")
        B.dma(chG, out_h[0:128, 4352:4872],
              h_all[:, 0:520], r=[RG(h_all)], tag="d6")
        B.dma(chG, out_h[0:16, 4872:8064], xmT[0:16, 0:3192], r=[RG(xmT)], tag="d7")
        B.barrier()
        sc6.close()
        sc5.close()
        sc4q.close()
        sc4p.close()
        B.finalize()
        B.run_block()
        stack.close()
        return nc
    B.barrier()
    sc6.close()
    sc5.close()
    sc4q.close()

    B.serial_on = True
    sc7 = Scope(nc)
    p_mm = [sc7.ps(f"p_mmG{j}", [128, 512]) for j in range(2)]
    x2t = [sc7.sb(f"x2t{j}", [128, DC * T3B], F32) for j in range(2)]
    xbt2 = [sc7.sb(f"xbt2{j}", [128, DC * T3B], F32) for j in range(2)]
    xpt2 = [sc7.sb(f"xpt2{j}", [128, DC * T3B], BF16) for j in range(2)]
    zt2 = [sc7.sb(f"zt2{j}", [128, DC * T3B], BF16) for j in range(2)]
    yF3 = sc7.sb("yF3", [128, DC * T3B], F32)
    g163 = sc7.sb("g163", [128, DC * T3B], BF16)
    x3_tgt = out_h if BUILD_UPTO in ("x3", "x3a", "x3b", "x3c") else x3s
    x3_v = fm_view(x3_tgt)
    for i in range(NT3B):
        b = i % 2
        xf, xbf, xpf, zf = x2t[b], xbt2[b], xpt2[b], zt2[b]
        B.dma(chL, xf[:].rearrange("p (c n) -> p c n", c=DC),
              x2_v[:, :, i * T3B:(i + 1) * T3B], w=[RG(xf)], tag=f"4x2{i}")
        B.dma(chL2, xbf[:].rearrange("p (c n) -> p c n", c=DC),
              xbs_v[:, :, i * T3B:(i + 1) * T3B], w=[RG(xbf)], tag=f"4xb{i}")
        B.dma(chL3, xpf[:].rearrange("p (c n) -> p c n", c=DC),
              xpsp_v[:, :, i * T3B:(i + 1) * T3B], w=[RG(xpf)], tag=f"4xp{i}")
        B.dma(chL4, zf[:].rearrange("p (c n) -> p c n", c=DC),
              zsp_v[:, :, i * T3B:(i + 1) * T3B], w=[RG(zf)], tag=f"4z{i}")
        for oc in range(DC):
            pf = p_mm[oc % 2]
            B.op("t", lambda h, i=i, oc=oc, pf=pf: h.matmul(
                pf[:, 0:T3B], wcb[:, oc * 128:(oc + 1) * 128],
                xmT[0:16, i * T3B:(i + 1) * T3B],
                start=True, stop=True),
                r=[RG(wcb), RG(xmT)], w=[RG(pf)], tag=f"4wc{i}", c=330)
            B.op("v", lambda h, oc=oc, pf=pf, xpf=xpf: h.scalar_tensor_tensor(
                yF3[:, oc * T3B:(oc + 1) * T3B], xpf[:, oc * T3B:(oc + 1) * T3B],
                cons["dskT"][:, oc:oc + 1], pf[:, 0:T3B], ALU.mult, ALU.add),
                r=[RG(pf), RG(xpf), RG(cons["dskT"])], w=[RG(yF3)], tag=f"4dsk{i}", c=690)
            B.op("v", lambda h, oc=oc, zf=zf: h.tensor_mul(
                g163[:, oc * T3B:(oc + 1) * T3B], yF3[:, oc * T3B:(oc + 1) * T3B],
                zf[:, oc * T3B:(oc + 1) * T3B]),
                r=[RG(yF3), RG(zf)], w=[RG(g163)], tag=f"4gt{i}", c=690)
        if BUILD_UPTO == "x3c":
            for dc in range(DC):
                B.op("v", lambda h, dc=dc, xf=xf, xbf=xbf: h.tensor_add(
                    xf[:, dc * T3B:(dc + 1) * T3B], xf[:, dc * T3B:(dc + 1) * T3B],
                    xbf[:, dc * T3B:(dc + 1) * T3B]),
                    r=[RG(xbf)], w=[RG(xf)], tag=f"4ca{i}")
            B.dma(chS, x3_v[:, :, i * T3B:(i + 1) * T3B],
                  xf[:].rearrange("p (c n) -> p c n", c=DC), r=[RG(xf)], tag=f"4st{i}")
            continue
        if BUILD_UPTO == "x3b":
            for oc in range(DC):
                pf = p_mm[oc % 2]
                B.op("t", lambda h, i=i, oc=oc, pf=pf: h.matmul(
                    pf[:, 0:T3B], wcb[:, oc * 128:(oc + 1) * 128],
                    xmT[0:16, i * T3B:(i + 1) * T3B],
                    start=True, stop=True),
                    r=[RG(wcb), RG(xmT)], w=[RG(pf)], tag=f"4wcb{i}", c=330)
                B.op("v", lambda h, oc=oc, pf=pf, xf=xf: h.tensor_copy(
                    xf[:, oc * T3B:(oc + 1) * T3B], pf[:, 0:T3B]),
                    r=[RG(pf)], w=[RG(xf)], tag=f"4cpb{i}")
            B.dma(chS, x3_v[:, :, i * T3B:(i + 1) * T3B],
                  xf[:].rearrange("p (c n) -> p c n", c=DC), r=[RG(xf)], tag=f"4st{i}")
            continue
        if BUILD_UPTO == "x3a":
            for dc in range(DC):
                B.op("v", lambda h, dc=dc, xf=xf: h.tensor_copy(
                    xf[:, dc * T3B:(dc + 1) * T3B], g163[:, dc * T3B:(dc + 1) * T3B]),
                    r=[RG(g163)], w=[RG(xf)], tag=f"4cp{i}")
            B.dma(chS, x3_v[:, :, i * T3B:(i + 1) * T3B],
                  xf[:].rearrange("p (c n) -> p c n", c=DC), r=[RG(xf)], tag=f"4st{i}")
            continue
        for oc in range(DC):
            pw = p_mm[oc % 2]
            for fc in range(DC):
                B.op("t", lambda h, oc=oc, fc=fc, pw=pw: h.matmul(
                    pw[:, 0:T3B],
                    wbo[:, fc * D + oc * 128: fc * D + (oc + 1) * 128],
                    g163[:, fc * T3B:(fc + 1) * T3B],
                    start=(fc == 0), stop=(fc == DC - 1)),
                    r=[RG(wbo), RG(g163)], w=[RG(pw)], tag=f"4wo{i}", c=330,
                    grp=RG(pw), gstart=(fc == 0))
            B.op("v", lambda h, oc=oc, pw=pw, xbf=xbf: h.scalar_tensor_tensor(
                yF3[:, oc * T3B:(oc + 1) * T3B], pw[:, 0:T3B],
                cons["boutT"][:, oc:oc + 1], xbf[:, oc * T3B:(oc + 1) * T3B],
                ALU.add, ALU.add),
                r=[RG(pw), RG(xbf), RG(cons["boutT"])], w=[RG(yF3)], tag=f"4bo{i}", c=690)
            B.op("v", lambda h, oc=oc, xf=xf: h.tensor_add(
                xf[:, oc * T3B:(oc + 1) * T3B], xf[:, oc * T3B:(oc + 1) * T3B],
                yF3[:, oc * T3B:(oc + 1) * T3B]),
                r=[RG(yF3)], w=[RG(xf)], tag=f"4x3{i}", c=690)
        B.dma(chS, x3_v[:, :, i * T3B:(i + 1) * T3B],
              xf[:].rearrange("p (c n) -> p c n", c=DC), r=[RG(xf)], tag=f"4st{i}")
    B.serial_on = False
    B.serial_prev = None
    B.barrier()
    sc7.close()
    sc4p.close()
    if BUILD_UPTO in ("x3", "x3a", "x3b", "x3c"):
        B.finalize()
        B.run_block()
        stack.close()
        return nc

    scf2 = Scope(nc)
    ffn_phase(fm_view(x3s), fm_view(out_h), scf2, "f2")
    B.barrier()
    scf2.close()

    B.finalize()
    B.run_block()
    stack.close()
    return nc


_NC_CACHE = {}


def _host_prep(inputs):
    f = np.ascontiguousarray
    x = np.asarray(inputs["L0_attn_a_x"], dtype=np.float32)
    w_in = np.asarray(inputs["L0_attn_a_w_in"], np.float32)
    w_out = np.asarray(inputs["L0_attn_a_w_out"], np.float32)
    beta = np.asarray(inputs["L0_attn_a_beta"], np.float32)
    A_log = np.asarray(inputs["L0_attn_b_A_log"], np.float32)
    W_B = np.asarray(inputs["L0_attn_b_W_B"], np.float32)
    W_C = np.asarray(inputs["L0_attn_b_W_C"], np.float32)
    D_skip = np.asarray(inputs["L0_attn_b_D_skip"], np.float32)
    log_dt = np.asarray(inputs["L0_attn_b_log_dt"], np.float32)
    bW_in = np.asarray(inputs["L0_attn_b_W_in"], np.float32)
    b_in = np.asarray(inputs["L0_attn_b_b_in"], np.float32)
    bW_out = np.asarray(inputs["L0_attn_b_W_out"], np.float32)
    b_out = np.asarray(inputs["L0_attn_b_b_out"], np.float32)
    ln_g = np.asarray(inputs["L0_attn_b_ln_g"], np.float32)
    ln_b = np.asarray(inputs["L0_attn_b_ln_b"], np.float32)
    qw = np.asarray(inputs["L0_ffn_qweights"], np.int32)
    scales = np.asarray(inputs["L0_ffn_scales"], np.float32)
    biases = np.asarray(inputs["L0_ffn_biases"], np.float32)
    lora_a = np.asarray(inputs["L0_ffn_lora_a"], np.float32)
    lora_b = np.asarray(inputs["L0_ffn_lora_b"], np.float32)
    ln_gamma = np.asarray(inputs["L0_ffn_ln_gamma"], np.float32)
    ln_beta = np.asarray(inputs["L0_ffn_ln_beta"], np.float32)

    shared = {
        "w_inT": f(w_in.T),
        "w_outT": f(w_out.T),
        "betaP": f(np.repeat(beta.reshape(8, 2, M), 64, axis=1)
                   .transpose(1, 0, 2).reshape(128, 88)),
        "qwT": f(qw.transpose(0, 2, 1)),
        "srT": f(np.repeat(scales, 16, axis=2).transpose(0, 2, 1)),
        "la": f(lora_a),
        "lbT": f(lora_b.transpose(0, 2, 1)),
        "biasF": f(biases.reshape(6, 8, 128).transpose(0, 2, 1)),
        "gamF": f(ln_gamma.reshape(3, 8, 128).transpose(0, 2, 1)),
        "betF": f(ln_beta.reshape(3, 8, 128).transpose(0, 2, 1)),
        "bwinT": f(bW_in.T),
        "bwoutT": f(bW_out.T),
        "wbT": f(W_B.T),
        "wcT": f(W_C.T),
        "binT": f(b_in.reshape(16, 128).T),
        "boutT": f(b_out.reshape(8, 128).T),
        "lngT": f(ln_g.reshape(8, 128).T),
        "lnbT": f(ln_b.reshape(8, 128).T),
        "dskT": f(D_skip.reshape(8, 128).T),
        "ldtT": f(log_dt.reshape(8, 128).T),
        "alogT": f(A_log.reshape(8, 128, 16).transpose(1, 0, 2).reshape(128, 128)),
        "identF": np.eye(128, dtype=np.float32),
        "triuF": f(np.triu(np.ones((128, 128), np.float32))),
        "onesk": np.full((128, 128), 1.0 / 1024.0, np.float32),
        "onesr": np.ones((1, 128), np.float32),
        "iotaB": f(np.tile(np.arange(132, dtype=np.float32), (128, 1))),
        "iotaP": f(np.stack([np.arange(128, dtype=np.float32),
                             127.0 - np.arange(128, dtype=np.float32)], axis=1)),
    }
    per_core = [dict(shared, x0T=f(x[c].T)) for c in range(x.shape[0])]
    return per_core, x.shape


def kernel(**inputs):
    per_core, xshape = _host_prep(inputs)
    key = BUILD_UPTO
    if key not in _NC_CACHE:
        _NC_CACHE[key] = build_program()
    nc = _NC_CACHE[key]
    res = run_bass_kernel_spmd(nc, per_core, core_ids=list(range(N_CORES)))
    out = np.stack([np.asarray(r["out"]).T for r in res.results], axis=0)
    return out.astype(np.float32)
